# version 01
# baseline (speedup 1.0000x reference)
import math
from contextlib import ExitStack

import numpy as np
import ml_dtypes

import concourse.bass as bass
import concourse.mybir as mybir
from concourse.bass_utils import run_bass_kernel_spmd

F32 = mybir.dt.float32
BF16 = mybir.dt.bfloat16
AF = mybir.ActivationFunctionType
ALU = mybir.AluOpType
AX = mybir.AxisListType

BF = ml_dtypes.bfloat16

B = 8
T = 4096
C = 1024
NH = 16
HD = 64
CT = C // 128
MOFF = 31
MCOLS = MOFF + T + 1
EPS = 1e-5

POLY_A = float(np.float32(math.sqrt(1e-3)))
POLY_S = -1e-5
POLY_C = 1e-3
C_R = float(np.float32(1e-5 / np.float32(math.sqrt(1e-3))))

ENGINES = ("tensor", "vector", "scalar", "gpsimd", "sync")


class Rec:

    def __init__(self, nc, sems, make_sem=None):
        self.nc = nc
        self.sems = sems
        self.make_sem = make_sem
        self.q = {e: [] for e in ENGINES}
        self.cnt = {e: 0 for e in ENGINES}
        self.cnt["cc"] = 0

    SAFE_ENGS = ("vector", "scalar", "gpsimd")

    def op(self, eng, meth, *a, inc=1, selfwait=None, sem=None, **k):
        if (selfwait is not False and eng in self.SAFE_ENGS
                and meth not in ("dma_start",) and self.cnt[eng] > 0):
            self.q[eng].append(("__wait__", (eng, self.cnt[eng]), {}, 0))
        tgt = sem or eng
        self.cnt[tgt] += inc
        self.q[eng].append((meth, a, k, (tgt, inc)))
        return self.cnt[tgt]

    def wait(self, eng, other, val):
        if val <= 0:
            return
        self.q[eng].append(("__wait__", (other, val), {}, 0))

    def dma(self, eng, out, in_, slot, **k):
        key = ("d", slot)
        if key not in self.sems:
            self.sems[key] = self.make_sem(f"ds_{slot}")
            self.cnt[key] = 0
        self.op(eng, "dma_start", inc=16, sem=key, out=out, in_=in_, **k)
        return (key, self.cnt[key])

    def waitd(self, eng, token):
        key, val = token
        self.wait(eng, key, val)

    def mm(self, out, lhsT, rhs, start, stop=False):
        return self.op("tensor", "matmul", out, lhsT, rhs,
                       start=start, stop=stop, skip_group_check=True)

    def join(self):
        fin = {e: self.cnt[e] for e in ENGINES}
        for e in ENGINES:
            for o in ENGINES:
                if o != e:
                    self.wait(e, o, fin[o])

    def flush_dmas(self):
        for key, v in self.cnt.items():
            if isinstance(key, tuple) and key[0] == "d" and v > 0:
                self.wait("sync", key, v)

    def replay(self, block):
        sems = self.sems
        for e in ENGINES:
            ops = self.q[e]
            if not ops:
                continue

            def make(ops, e):
                def f(eng):
                    for meth, a, k, inc in ops:
                        if meth == "__wait__":
                            other, val = a
                            eng.wait_ge(sems[other], val)
                        else:
                            ins = getattr(eng, meth)(*a, **k)
                            if inc:
                                tgt, n = inc if isinstance(inc, tuple) else (e, inc)
                                if n:
                                    ins.then_inc(sems[tgt], n)
                return f

            getattr(block, e)(make(ops, e))
        self.q = {e: [] for e in ENGINES}


class Gate:

    def __init__(self):
        self.need = {}

    def use(self, eng, cnt):
        self.need[eng] = max(self.need.get(eng, 0), cnt)

    def use_tok(self, tok):
        self.use(tok[0], tok[1])

    def transfer_to(self, r, writer):
        self.gate(r, writer)

    def gate(self, r, writer):
        for eng, cnt in self.need.items():
            if eng != writer:
                r.wait(writer, eng, cnt)
        self.need = {}


class H:
    pass



def host_constants():
    c = {}
    c["ident_f32"] = np.eye(128, dtype=np.float32)
    c["ident_bf"] = np.eye(128, dtype=BF)
    c["ones_col_f32"] = np.ones((128, 1), np.float32)
    c["ones_col_bf"] = np.ones((128, 1), BF)
    c["ones_row_f32"] = np.ones((1, 512), np.float32)
    c["zeros_col_f32"] = np.zeros((128, 1), np.float32)
    c["eps_col"] = np.full((128, 1), EPS, np.float32)
    p = np.arange(128)[:, None]
    col = np.arange(512)[None, :]
    m = [(col >= p + off * 128) for off in range(4)]
    c["l0_mask"] = np.concatenate(m, axis=1).astype(BF)
    c["l0_cmask"] = (np.concatenate(m, axis=1).astype(np.float32) * POLY_C).astype(BF)
    j = np.arange(128)[:, None]
    cc = np.arange(128)[None, :]
    kw = j - 32
    wA = np.floor_divide(kw, 16)
    wQ = np.floor_divide(cc, 16)
    mA = (wA >= wQ - 2) & (kw <= cc)
    c["l1_maskA"] = mA.astype(BF)
    c["l1_maskA0"] = (mA & (kw >= 0)).astype(BF)
    jB = np.arange(32)[:, None]
    kwB = 96 + jB
    mB = (kwB <= cc)
    c["l1_maskB"] = mB.astype(BF)
    jj = np.arange(160)[None, :]
    kw2 = jj - 32
    wA2 = np.floor_divide(kw2, 16)
    qq = np.arange(128)[:, None]
    wQ2 = np.floor_divide(qq, 16)
    valid = (wA2 >= wQ2 - 2) & (kw2 <= qq)
    valid0 = valid & (kw2 >= 0)
    am = np.zeros((128, 2 * 176), np.float32)
    am[:, 0:160] = np.where(valid, 0.0, -1e9)
    am[:, 176:336] = np.where(valid0, 0.0, -1e9)
    c["l1_amask"] = am.astype(BF)
    return c


def host_prep(inputs):
    ws = {}
    ws["W_attn"] = np.ascontiguousarray(inputs["L0_attn_W_attn"])
    ws["W_proj"] = np.ascontiguousarray(inputs["L0_attn_W_proj"])
    ws["gate_wT"] = np.ascontiguousarray(inputs["L0_ffn_gate_w"].T)
    ws["W1T"] = np.ascontiguousarray(inputs["L0_ffn_W1"].transpose(0, 2, 1))
    ws["W2T"] = np.ascontiguousarray(inputs["L0_ffn_W2"].transpose(0, 2, 1))
    ws["Ws1T"] = np.ascontiguousarray(inputs["L0_ffn_Ws1"].T)
    ws["Ws2T"] = np.ascontiguousarray(inputs["L0_ffn_Ws2"].T)
    ws["w_qkv"] = np.ascontiguousarray(inputs["L1_attn_w_qkv"])
    ws["w_out"] = np.ascontiguousarray(inputs["L1_attn_w_out"])
    ws["b_outT"] = np.ascontiguousarray(inputs["L1_attn_b_out"].reshape(CT, 128).T)
    w_th = inputs["L1_attn_w_th"]
    wmix = np.zeros((C, C), np.float32)
    d = np.arange(HD)
    for gidx in range(NH):
        for hh in range(NH):
            wmix[hh * HD + d, gidx * HD + d] = w_th[gidx, hh]
    ws["W_mix"] = wmix
    kron = np.zeros((128, NH * 4 * 512), np.float32)
    q_ = np.arange(128)
    for hh in range(NH):
        for gb in range(4):
            for gi in range(4):
                gidx = gb * 4 + gi
                kron[q_, (hh * 4 + gb) * 512 + gi * 128 + q_] = w_th[gidx, hh]
    ws["wth_kron"] = kron
    ws["phi"] = np.ascontiguousarray(inputs["L1_ffn_phi"].reshape(C, 2048))
    ws["Wl1"] = np.ascontiguousarray(inputs["L1_ffn_W"])
    bTfull = inputs["L1_ffn_b"].T
    ws["bT"] = np.ascontiguousarray(
        bTfull.reshape(CT, 128, 64).transpose(1, 0, 2).reshape(128, CT * 64))
    return ws


CONST_SPECS = [
    ("ident_f32", (128, 128), F32),
    ("ident_bf", (128, 128), BF16),
    ("ones_col_f32", (128, 1), F32),
    ("ones_col_bf", (128, 1), BF16),
    ("ones_row_f32", (1, 512), F32),
    ("zeros_col_f32", (128, 1), F32),
    ("eps_col", (128, 1), F32),
    ("l0_mask", (128, 2048), BF16),
    ("l0_cmask", (128, 2048), BF16),
    ("l1_maskA", (128, 128), BF16),
    ("l1_maskA0", (128, 128), BF16),
    ("l1_maskB", (32, 128), BF16),
    ("l1_amask", (128, 352), BF16),
]



def sbuf(ctx, nc, name, shape, dtype):
    return ctx.enter_context(nc.sbuf_tensor(name, list(shape), dtype))



def p1_l0_attn(nc, r, ctx, g):
    ma, mb = g.master_a, g.master_b
    x_in = g.x_in
    P = g.psum

    xq = sbuf(ctx, nc, "p1_xq", (128, 8 * 512), BF16)
    wq = sbuf(ctx, nc, "p1_wq", (128, 8 * 1024), BF16)
    wp = sbuf(ctx, nc, "p1_wp", (128, 8 * 1024), BF16)
    zstub = sbuf(ctx, nc, "p1_zero", (128, 64), F32)

    bank_gate = [Gate() for _ in range(8)]
    buf_gate = {}
    ma_wr = []

    def bg(name):
        if name not in buf_gate:
            buf_gate[name] = Gate()
        return buf_gate[name]

    r.op("gpsimd", "memset", zstub[:, :], 0.0)
    for c in range(CT):
        wtok = r.dma("gpsimd", wq[:, c * 1024:(c + 1) * 1024],
                     g.W_attn[c * 128:(c + 1) * 128, 0:1024], slot="p1w")
        wtok = r.dma("gpsimd", wp[:, c * 1024:(c + 1) * 1024],
                     g.W_proj[c * 128:(c + 1) * 128, :], slot="p1w")

    r.wait("sync", "gpsimd", 1)
    for m in (ma, mb):
        for c in range(CT):
            r.dma("sync", m[c * 128:(c + 1) * 128, 0:MOFF], zstub[:, 0:MOFF],
                  slot="p1mz")
            r.dma("sync", m[c * 128:(c + 1) * 128, MCOLS - 1:MCOLS], zstub[:, 0:1],
                  allow_slow_non_contiguous=True, slot="p1mz")

    with ExitStack() as actx:
        xin = sbuf(actx, nc, "p1_xin", (128, C), F32)
        xstage = sbuf(actx, nc, "p1_xstage", (128, 8 * 512), F32)
        wk = sbuf(actx, nc, "p1_wk", (128, 8 * 1024), BF16)
        wv = sbuf(actx, nc, "p1_wv", (128, 8 * 1024), BF16)
        kstage = sbuf(actx, nc, "p1_kstage", (64, NH * 512), BF16)
        vstage = sbuf(actx, nc, "p1_vstage", (128, C), BF16)
        v2stage = sbuf(actx, nc, "p1_v2stage", (128, C), BF16)
        kkstage = sbuf(actx, nc, "p1_kkstage", (128, C), BF16)

        for c in range(CT):
            wtok = r.dma("gpsimd", wk[:, c * 1024:(c + 1) * 1024],
                         g.W_attn[c * 128:(c + 1) * 128, 1024:2048], slot="p1w")
            wtok = r.dma("gpsimd", wv[:, c * 1024:(c + 1) * 1024],
                         g.W_attn[c * 128:(c + 1) * 128, 2048:3072], slot="p1w")
        wload = wtok

        for tg in range(8):
            T0 = tg * 512
            for i in range(4):
                bg("xin").gate(r, "gpsimd")
                ld = r.dma("gpsimd", xin[:, :],
                           x_in[T0 + i * 128:T0 + (i + 1) * 128, :], slot="p1xin")
                r.waitd("tensor", ld)
                if tg == 0 and i == 0:
                    r.waitd("tensor", wload)
                for d in range(CT):
                    pb = d % 2
                    bank_gate[pb].gate(r, "tensor")
                    tc = r.op("tensor", "transpose", P[pb][:, 0:128],
                              xin[:, d * 128:(d + 1) * 128], g.ident_f32[:, :])
                    bg("xin").use("tensor", tc)
                    r.wait("vector", "tensor", tc)
                    cc = r.op("vector", "tensor_copy",
                              xstage[:, d * 512 + i * 128:d * 512 + (i + 1) * 128],
                              P[pb][:, 0:128])
                    bank_gate[pb].use("vector", cc)
            vdone = r.cnt["vector"]
            r.wait("sync", "vector", vdone)
            for c in range(CT):
                wr = r.dma("sync", ma[c * 128:(c + 1) * 128, MOFF + T0:MOFF + T0 + 512],
                           xstage[:, c * 512:(c + 1) * 512], slot="p1maw")
                bg("xstage").use_tok(wr)
            ma_wr.append(wr)
            r.wait("scalar", "vector", vdone)
            bg("xq").gate(r, "scalar")
            for c in range(CT):
                sc = r.op("scalar", "activation", xq[:, c * 512:(c + 1) * 512],
                          xstage[:, c * 512:(c + 1) * 512], AF.Copy)
                bg("xstage").use("scalar", sc)
            castdone = r.cnt["scalar"]
            bg("xstage").gate(r, "vector")

            r.wait("tensor", "scalar", castdone)
            for f in range(CT):
                pb = 2 + (f % 2)
                bank_gate[pb].gate(r, "tensor")
                for c in range(CT):
                    mmc = r.mm(P[pb][:, :],
                               wk[:, c * 1024 + f * 128:c * 1024 + (f + 1) * 128],
                               xq[:, c * 512:(c + 1) * 512],
                               start=(c == 0), stop=(c == CT - 1))
                    bg("xq").use("tensor", mmc)
                r.wait("vector", "tensor", mmc)
                c0 = r.op("vector", "tensor_copy",
                          kstage[0:64, (2 * f) * 512:(2 * f) * 512 + 512],
                          P[pb][0:64, :])
                c1 = r.op("vector", "tensor_copy",
                          kstage[0:64, (2 * f + 1) * 512:(2 * f + 1) * 512 + 512],
                          P[pb][64:128, :])
                bank_gate[pb].use("vector", c1)
            kev = r.cnt["vector"]
            r.wait("sync", "vector", kev)
            ksrc = kstage[:, :].rearrange("p (h q) -> p h q", h=NH)
            for i in range(4):
                jt = tg * 4 + i
                dst = g.kT_hbm[jt:jt + 1, :, :].rearrange("o p c -> (o p) c") \
                    .rearrange("p (h q) -> p h q", h=NH)
                ktw = r.dma("sync", dst, ksrc[:, :, i * 128:(i + 1) * 128],
                            slot="p1ktw")
                bg("kstage").use_tok(ktw)
            bg("kstage").gate(r, "vector")

            for i in range(4):
                t128 = T0 + i * 128
                for which, wmat, stage, hbm in (
                    (0, wv, vstage, g.v_hbm),
                    (1, wk, kkstage, g.k_hbm),
                ):
                    bg(("stg", which)).gate(r, "vector")
                    if which == 0:
                        bg("stg2").gate(r, "scalar")
                    for n in range(2):
                        pb = 4 + n
                        bank_gate[pb].gate(r, "tensor")
                        for c in range(CT):
                            mmc = r.mm(P[pb][:, :],
                                       xq[:, c * 512 + i * 128:c * 512 + (i + 1) * 128],
                                       wmat[:, c * 1024 + n * 512:c * 1024 + (n + 1) * 512],
                                       start=(c == 0), stop=(c == CT - 1))
                            bg("xq").use("tensor", mmc)
                        r.wait("vector", "tensor", mmc)
                        cc = r.op("vector", "tensor_copy",
                                  stage[:, n * 512:(n + 1) * 512], P[pb][:, :])
                        bank_gate[pb].use("vector", cc)
                        if which == 0:
                            r.wait("scalar", "vector", cc)
                            c2 = r.op("scalar", "activation",
                                      v2stage[:, n * 512:(n + 1) * 512],
                                      stage[:, n * 512:(n + 1) * 512],
                                      AF.Copy, scale=C_R)
                            bg(("stg", 0)).use("scalar", c2)
                    r.wait("sync", "vector", cc)
                    vkw = r.dma("sync", hbm[t128:t128 + 128, :], stage[:, :],
                                slot="p1vw" if which == 0 else "p1kw")
                    bg(("stg", which)).use_tok(vkw)
                    bg(("stg", which)).gate(r, "vector")
                    if which == 0:
                        r.wait("sync", "scalar", c2)
                        v2w = r.dma("sync", g.v2_hbm[t128:t128 + 128, :],
                                    v2stage[:, :], slot="p1v2w")
                        bg("stg2").use_tok(v2w)
                        bg("stg2").gate(r, "scalar")
        stage_a_done = [("d", sl) for sl in
                        ("p1maw", "p1ktw", "p1vw", "p1kw", "p1v2w", "p1mz")]
        r.join()
        for key in stage_a_done:
            for e in ("tensor", "sync", "gpsimd"):
                r.wait(e, key, r.cnt.get(key, 0))

    with ExitStack() as bctx:
        NSLOT = 3
        vring = [sbuf(bctx, nc, f"p1_vring{i}", (128, C), BF16) for i in range(NSLOT)]
        v2ring = [sbuf(bctx, nc, f"p1_v2ring{i}", (128, C), BF16) for i in range(NSLOT)]
        kring = [sbuf(bctx, nc, f"p1_kring{i}", (64, NH * 128), BF16)
                 for i in range(NSLOT)]
        ktok = [sbuf(bctx, nc, f"p1_ktok{i}", (128, C), BF16) for i in range(2)]
        vtok = [sbuf(bctx, nc, f"p1_vtok{i}", (128, C), BF16) for i in range(2)]
        qts = sbuf(bctx, nc, "p1_qts", (64, NH * 512), BF16)
        msb = sbuf(bctx, nc, "p1_msb", (64, NH * HD), BF16)
        rbuf = [sbuf(bctx, nc, f"p1_rb{i}", (128, 512), BF16) for i in range(2)]
        rmbuf = [sbuf(bctx, nc, f"p1_rm{i}", (128, 512), BF16) for i in range(2)]
        abuf = [sbuf(bctx, nc, f"p1_ab{i}", (128, 512), BF16) for i in range(2)]
        s16a = sbuf(bctx, nc, "p1_s16a", (128, 512), BF16)
        s16b = sbuf(bctx, nc, "p1_s16b", (128, 512), BF16)
        rprime = sbuf(bctx, nc, "p1_rp", (128, 8), F32)
        m_f32 = sbuf(bctx, nc, "p1_mf32", (64, NH * HD), F32)
        r_f32 = sbuf(bctx, nc, "p1_rf32", (128, 8), F32)
        outT = sbuf(bctx, nc, "p1_outT", (128, 8 * 512), BF16)
        xold = sbuf(bctx, nc, "p1_xold", (128, 8 * 512), F32)
        tsb = sbuf(bctx, nc, "p1_t", (128, 8 * 512), F32)
        sqb = sbuf(bctx, nc, "p1_sq", (128, 512), BF16)
        st1 = sbuf(bctx, nc, "p1_st1", (1, 4 * 512), F32)
        xnew = sbuf(bctx, nc, "p1_xnew", (128, 512), F32)

        mr_snap = 0
        rp_zero = True
        r.op("gpsimd", "memset", rprime[:, :], 0.0)
        r.op("gpsimd", "memset", m_f32[:, :], 0.0)
        mz = r.op("gpsimd", "memset", r_f32[:, :], 0.0)
        r.wait("vector", "gpsimd", mz)

        for qt in range(8):
            Q0 = qt * 512
            njt = 4 * qt + 4

            bg("xq").gate(r, "gpsimd")
            for c in range(CT):
                xqt = r.dma("gpsimd", xq[:, c * 512:(c + 1) * 512],
                            ma[c * 128:(c + 1) * 128,
                               MOFF + Q0:MOFF + Q0 + 512], slot="p1xq")
            r.waitd("tensor", xqt)
            bg("qts").gate(r, "scalar")
            for f in range(CT):
                pb = f % 2
                bank_gate[pb].gate(r, "tensor")
                for c in range(CT):
                    mmc = r.mm(P[pb][:, :],
                               wq[:, c * 1024 + f * 128:c * 1024 + (f + 1) * 128],
                               xq[:, c * 512:(c + 1) * 512],
                               start=(c == 0), stop=(c == CT - 1))
                    bg("xq").use("tensor", mmc)
                r.wait("scalar", "tensor", mmc)
                s0 = r.op("scalar", "activation",
                          qts[0:64, (2 * f) * 512:(2 * f + 1) * 512],
                          P[pb][0:64, :], AF.Copy, scale=0.125)
                s1 = r.op("scalar", "activation",
                          qts[0:64, (2 * f + 1) * 512:(2 * f + 2) * 512],
                          P[pb][64:128, :], AF.Copy, scale=0.125)
                bank_gate[pb].use("scalar", s1)
                bg("qts").use("scalar", s1)
            qts_ready = r.cnt["scalar"]

            bg("xold").gate(r, "sync")
            for c in range(CT):
                xold_ready = r.dma("sync", xold[:, c * 512:(c + 1) * 512],
                                   ma[c * 128:(c + 1) * 128,
                                      MOFF + Q0:MOFF + Q0 + 512], slot="p1xold")

            if qt > 0:
                r.wait("tensor", "vector", mr_snap)
                bank_gate[0].gate(r, "tensor")
                bank_gate[1].gate(r, "tensor")
                bank_gate[2].gate(r, "tensor")
                for jj in range(4):
                    jt = 4 * (qt - 1) + jj
                    sl = jj % 2
                    bg(("ktok", sl)).gate(r, "sync")
                    l1 = r.dma("sync", ktok[sl][:, :],
                               g.k_hbm[jt * 128:(jt + 1) * 128, :],
                               slot=f"p1kt{sl}")
                    bg(("vtok", sl)).gate(r, "sync")
                    l2 = r.dma("sync", vtok[sl][:, :],
                               g.v_hbm[jt * 128:(jt + 1) * 128, :],
                               slot=f"p1vt{sl}")
                    r.waitd("tensor", l1)
                    r.waitd("tensor", l2)
                    for hh in range(NH):
                        tb = P[0] if hh < 8 else P[1]
                        col = (hh % 8) * 64
                        mmc = r.mm(tb[0:64, col:col + 64],
                                   ktok[sl][:, hh * 64:(hh + 1) * 64],
                                   vtok[sl][:, hh * 64:(hh + 1) * 64],
                                   start=(jj == 0 and hh % 8 == 0))
                        bg(("ktok", sl)).use("tensor", mmc)
                    for hp in range(8):
                        mmc = r.mm(P[2][:, hp:hp + 1],
                                   vtok[sl][:, hp * 128:(hp + 1) * 128],
                                   g.ones_col_bf[:, :], start=(jj == 0 and hp == 0))
                        bg(("vtok", sl)).use("tensor", mmc)
                mr_done = r.cnt["tensor"]
                r.wait("vector", "tensor", mr_done)
                v1 = r.op("vector", "tensor_tensor", m_f32[:, 0:512], m_f32[:, 0:512],
                          P[0][0:64, :], ALU.add)
                v2 = r.op("vector", "tensor_tensor", m_f32[:, 512:1024],
                          m_f32[:, 512:1024], P[1][0:64, :], ALU.add)
                v3 = r.op("vector", "tensor_tensor", r_f32[:, :], r_f32[:, :],
                          P[2][:, 0:8], ALU.add)
                bank_gate[0].use("vector", v1)
                bank_gate[1].use("vector", v2)
                bank_gate[2].use("vector", v3)
                bg("msb").gate(r, "vector")
                r.op("vector", "tensor_scalar", msb[:, :], m_f32[:, :], POLY_S, None,
                     op0=ALU.mult)
                bg("rprime").gate(r, "vector")
                vr = r.op("vector", "tensor_scalar", rprime[:, :], r_f32[:, :], POLY_C,
                          None, op0=ALU.mult)
                rp_zero = False
                mr_snap = vr
            msb_ready = r.cnt["vector"]

            for grp in range(2):
                hps = range(grp * 4, grp * 4 + 4)
                av_started = {(hp, h2): False for hp in hps for h2 in range(2)}
                if qt > 0:
                    r.wait("tensor", "vector", msb_ready)
                    r.wait("tensor", "scalar", qts_ready)
                    for hp in hps:
                        ab = P[2 + hp % 4]
                        bank_gate[2 + hp % 4].gate(r, "tensor")
                        for h2 in range(2):
                            hh = 2 * hp + h2
                            mmc = r.mm(ab[h2 * 64:(h2 + 1) * 64, :],
                                       msb[0:64, hh * 64:(hh + 1) * 64],
                                       qts[0:64, hh * 512:(hh + 1) * 512],
                                       start=(not av_started[hp, h2]))
                            av_started[hp, h2] = True
                            bg("msb").use("tensor", mmc)
                            bg("qts").use("tensor", mmc)
                else:
                    for hp in hps:
                        bank_gate[2 + hp % 4].gate(r, "tensor")

                ring_loaded = {}

                def load_jt(jj):
                    sl = jj % NSLOT
                    bg(("vring", sl)).gate(r, "sync")
                    t1 = r.dma("sync", vring[sl][:, :],
                               g.v_hbm[jj * 128:(jj + 1) * 128, :],
                               slot=f"p1vr{sl}")
                    bg(("v2ring", sl)).gate(r, "sync")
                    t2 = r.dma("sync", v2ring[sl][:, :],
                               g.v2_hbm[jj * 128:(jj + 1) * 128, :],
                               slot=f"p1v2r{sl}")
                    bg(("kring", sl)).gate(r, "sync")
                    t3 = r.dma("sync", kring[sl][:, :],
                               g.kT_hbm[jj:jj + 1, :, :]
                               .rearrange("o p c -> (o p) c"),
                               slot=f"p1kr{sl}")
                    ring_loaded[jj] = (t1, t2, t3)

                for jj in range(min(NSLOT, njt)):
                    load_jt(jj)

                for jj in range(njt):
                    sl = jj % NSLOT
                    for t_ in ring_loaded[jj]:
                        r.waitd("tensor", t_)
                    if jj == 0:
                        r.wait("tensor", "scalar", qts_ready)
                    diag = jj >= 4 * qt
                    off = jj - 4 * qt
                    for hp in hps:
                        ab = P[2 + hp % 4]
                        for h2 in range(2):
                            hh = 2 * hp + h2
                            spb = h2
                            bank_gate[spb].gate(r, "tensor")
                            smm = r.mm(P[spb][:, :],
                                       kring[sl][0:64, hh * 128:(hh + 1) * 128],
                                       qts[0:64, hh * 512:(hh + 1) * 512],
                                       start=True, stop=True)
                            bg(("kring", sl)).use("tensor", smm)
                            bg("qts").use("tensor", smm)
                            r.wait("scalar", "tensor", smm)
                            bg(("rbuf", h2)).gate(r, "scalar")
                            sr = r.op("scalar", "activation", rbuf[h2][:, :],
                                      P[spb][:, :], AF.Relu, scale=POLY_A)
                            bank_gate[spb].use("scalar", sr)
                            r.wait("vector", "scalar", sr)
                            if diag:
                                bg(("rmbuf", h2)).gate(r, "vector")
                                r.op("vector", "tensor_tensor", rmbuf[h2][:, :],
                                     rbuf[h2][:, :],
                                     g.l0_mask[:, off * 512:(off + 1) * 512], ALU.mult)
                                asrc = rmbuf[h2]
                            else:
                                asrc = rbuf[h2]
                            bg(("abuf", h2)).gate(r, "vector")
                            va = r.op("vector", "tensor_tensor", abuf[h2][:, :],
                                      asrc[:, :], rbuf[h2][:, :], ALU.mult)
                            bg(("rbuf", h2)).use("vector", va)
                            if diag:
                                bg(("rmbuf", h2)).use("vector", va)
                                bg("s16a").gate(r, "scalar")
                                ss = r.op("scalar", "activation", s16a[:, :],
                                          P[spb][:, :], AF.Copy, scale=POLY_S)
                                bank_gate[spb].use("scalar", ss)
                                r.wait("vector", "scalar", ss)
                                bg("s16b").gate(r, "vector")
                                vs = r.op("vector", "tensor_tensor", s16b[:, :],
                                          s16a[:, :],
                                          g.l0_mask[:, off * 512:(off + 1) * 512],
                                          ALU.mult)
                                bg("s16a").use("vector", vs)
                            r.wait("tensor", "vector", va)
                            amm = r.mm(ab[h2 * 64:(h2 + 1) * 64, :],
                                       vring[sl][:, hh * 64:(hh + 1) * 64],
                                       abuf[h2][:, :], start=(not av_started[hp, h2]))
                            av_started[hp, h2] = True
                            bg(("abuf", h2)).use("tensor", amm)
                            bg(("vring", sl)).use("tensor", amm)
                            rmm = r.mm(ab[h2 * 64:(h2 + 1) * 64, :],
                                       v2ring[sl][:, hh * 64:(hh + 1) * 64],
                                       asrc[:, :], start=False)
                            bg(("v2ring", sl)).use("tensor", rmm)
                            if diag:
                                bg(("rmbuf", h2)).use("tensor", rmm)
                            else:
                                bg(("rbuf", h2)).use("tensor", rmm)
                            if diag:
                                r.wait("tensor", "vector", vs)
                                smm2 = r.mm(ab[h2 * 64:(h2 + 1) * 64, :],
                                            vring[sl][:, hh * 64:(hh + 1) * 64],
                                            s16b[:, :], start=False)
                                bg("s16b").use("tensor", smm2)
                                cmm = r.mm(ab[h2 * 64:(h2 + 1) * 64, :],
                                           vring[sl][:, hh * 64:(hh + 1) * 64],
                                           g.l0_cmask[:, off * 512:(off + 1) * 512],
                                           start=False)
                                bg(("vring", sl)).use("tensor", cmm)
                    if jj + NSLOT < njt:
                        load_jt(jj + NSLOT)

                avdone = r.cnt["tensor"]
                r.wait("vector", "tensor", avdone)
                for hp in hps:
                    ab = P[2 + hp % 4]
                    sc1 = rprime[:, hp:hp + 1] if not rp_zero else g.zeros_col_f32[:, :]
                    bg("outT").gate(r, "vector")
                    ev = r.op("vector", "tensor_scalar",
                              outT[:, hp * 512:(hp + 1) * 512],
                              ab[:, :], sc1, None, op0=ALU.add)
                    bank_gate[2 + hp % 4].use("vector", ev)
                    bg("rprime").use("vector", ev)
            outT_ready = r.cnt["vector"]
            if g.dbg:
                r.wait("sync", "vector", outT_ready)
                for c in range(CT):
                    dtk = r.dma("sync", g.dbg_o[c * 128:(c + 1) * 128, Q0:Q0 + 512],
                                outT[:, c * 512:(c + 1) * 512], slot="p1dbg")
                    bg("outT").use_tok(dtk)

            r.wait("tensor", "vector", outT_ready)
            r.waitd("vector", xold_ready)
            for et in range(CT):
                pb = et % 2
                bank_gate[pb].gate(r, "tensor")
                for c in range(CT):
                    mmc = r.mm(P[pb][:, :],
                               wp[:, c * 1024 + et * 128:c * 1024 + (et + 1) * 128],
                               outT[:, c * 512:(c + 1) * 512],
                               start=(c == 0), stop=(c == CT - 1))
                    bg("outT").use("tensor", mmc)
                r.wait("vector", "tensor", mmc)
                tv = r.op("vector", "tensor_tensor", tsb[:, et * 512:(et + 1) * 512],
                          xold[:, et * 512:(et + 1) * 512], P[pb][:, :], ALU.add)
                bank_gate[pb].use("vector", tv)
                bg("xold").use("vector", tv)
                r.wait("tensor", "vector", tv)
                if et == 0:
                    bank_gate[2].gate(r, "tensor")
                    bank_gate[3].gate(r, "tensor")
                r.mm(P[2][0:1, :], g.ones_col_f32[:, :],
                     tsb[:, et * 512:(et + 1) * 512],
                     start=(et == 0), stop=(et == CT - 1))
                r.wait("scalar", "vector", tv)
                bg("sqb").gate(r, "scalar")
                sq = r.op("scalar", "activation", sqb[:, :],
                          tsb[:, et * 512:(et + 1) * 512], AF.Square)
                r.wait("tensor", "scalar", sq)
                qm = r.mm(P[3][0:1, :], g.ones_col_bf[:, :], sqb[:, :],
                          start=(et == 0), stop=(et == CT - 1))
                bg("sqb").use("tensor", qm)
                bg(("tsb", et)).use("tensor", qm)
            stat_mm = r.cnt["tensor"]
            r.wait("vector", "tensor", stat_mm)
            mean = st1[0:1, 0:512]
            q2 = st1[0:1, 512:1024]
            scr = st1[0:1, 1024:1536]
            rstd = st1[0:1, 1536:2048]
            r.op("vector", "tensor_scalar", mean, P[2][0:1, :], 1.0 / C, None,
                 op0=ALU.mult)
            vq = r.op("vector", "tensor_scalar", q2, P[3][0:1, :], 1.0 / C, None,
                      op0=ALU.mult)
            bank_gate[2].use("vector", vq)
            bank_gate[3].use("vector", vq)
            r.op("vector", "tensor_tensor", scr, mean, mean, ALU.mult)
            vv = r.op("vector", "tensor_tensor", q2, q2, scr, ALU.subtract)
            r.wait("scalar", "vector", vv)
            sl_ = r.op("scalar", "activation", scr, q2, AF.Ln,
                       bias=g.eps_col[0:1, 0:1])
            sr_ = r.op("scalar", "activation", rstd, scr, AF.Exp, scale=-0.5)
            r.wait("vector", "scalar", sr_)
            vn = r.op("vector", "scalar_tensor_tensor", scr, mean, -1.0, rstd,
                      op0=ALU.mult, op1=ALU.mult)
            r.wait("tensor", "vector", vn)
            bank_gate[4].gate(r, "tensor")
            bank_gate[5].gate(r, "tensor")
            r.mm(P[4][:, :], g.ones_row_f32[0:1, 0:128], rstd, start=True, stop=True)
            nm_mm = r.mm(P[5][:, :], g.ones_row_f32[0:1, 0:128], scr,
                         start=True, stop=True)
            r.wait("vector", "tensor", nm_mm)
            for et in range(CT):
                bg("xnew").gate(r, "vector")
                r.op("vector", "tensor_tensor", xnew[:, :],
                     tsb[:, et * 512:(et + 1) * 512], P[4][:, :], ALU.mult)
                v2 = r.op("vector", "tensor_tensor", xnew[:, :], xnew[:, :],
                          P[5][:, :], ALU.add)
                bg(("tsb", et)).use("vector", v2)
                bank_gate[4].use("vector", v2)
                bank_gate[5].use("vector", v2)
                r.wait("sync", "vector", v2)
                mbw = r.dma("sync", mb[et * 128:(et + 1) * 128,
                                       MOFF + Q0:MOFF + Q0 + 512], xnew[:, :],
                            slot="p1mbw")
                bg("xnew").use_tok(mbw)
                if g.dbg:
                    dtk = r.dma("sync", g.dbg_t[et * 128:(et + 1) * 128,
                                                Q0:Q0 + 512],
                                tsb[:, et * 512:(et + 1) * 512], slot="p1dbg")
                    bg(("tsb", et)).use_tok(dtk)

        r.flush_dmas()
        fin = {e: r.cnt[e] for e in ENGINES}
        for e in ENGINES:
            for o in ENGINES:
                if o != e:
                    r.wait(e, o, fin[o])



class LNKit:

    def __init__(self, nc, r, ctx, g, bg, bank_gate, pS, pQ, pR, pN, W=512,
                 name="ln"):
        self.nc, self.r, self.g = nc, r, g
        self.bg, self.bank_gate = bg, bank_gate
        self.pS, self.pQ, self.pR, self.pN = pS, pQ, pR, pN
        self.W = W
        self.sqb = sbuf(ctx, nc, f"{name}_sq", (128, W), BF16)
        self.st1 = sbuf(ctx, nc, f"{name}_st", (1, 4 * W), F32)
        self.xnew = sbuf(ctx, nc, f"{name}_xn", (128, W), F32)

    def stats_et(self, tsb_slice, et, tv, W=None):
        W = W or self.W
        r, g = self.r, self.g
        r.wait("tensor", "vector", tv)
        if et == 0:
            self.bank_gate[self.pS].gate(r, "tensor")
            self.bank_gate[self.pQ].gate(r, "tensor")
        P = g.psum
        r.mm(P[self.pS][0:1, 0:W], g.ones_col_f32[:, :], tsb_slice,
             start=(et == 0), stop=(et == CT - 1))
        r.wait("scalar", "vector", tv)
        self.bg("lnsqb").gate(r, "scalar")
        sq = r.op("scalar", "activation", self.sqb[:, 0:W], tsb_slice, AF.Square)
        r.wait("tensor", "scalar", sq)
        qm = r.mm(P[self.pQ][0:1, 0:W], g.ones_col_bf[:, :], self.sqb[:, 0:W],
                  start=(et == 0), stop=(et == CT - 1))
        self.bg("lnsqb").use("tensor", qm)
        return qm

    def finish(self, tsb, mout, mcol, tsb_uses=None, W=None, tstride=None,
               noncontig=False):
        r, g = self.r, self.g
        P = g.psum
        W = W or self.W
        tstride = tstride or W
        st1 = self.st1
        MW = self.W
        stat_mm = r.cnt["tensor"]
        r.wait("vector", "tensor", stat_mm)
        mean = st1[0:1, 0:W]
        q2 = st1[0:1, MW:MW + W]
        scr = st1[0:1, 2 * MW:2 * MW + W]
        rstd = st1[0:1, 3 * MW:3 * MW + W]
        r.op("vector", "tensor_scalar", mean, P[self.pS][0:1, 0:W], 1.0 / C, None,
             op0=ALU.mult)
        vq = r.op("vector", "tensor_scalar", q2, P[self.pQ][0:1, 0:W], 1.0 / C, None,
                  op0=ALU.mult)
        self.bank_gate[self.pS].use("vector", vq)
        self.bank_gate[self.pQ].use("vector", vq)
        r.op("vector", "tensor_tensor", scr, mean, mean, ALU.mult)
        vv = r.op("vector", "tensor_tensor", q2, q2, scr, ALU.subtract)
        r.wait("scalar", "vector", vv)
        r.op("scalar", "activation", scr, q2, AF.Ln, bias=g.eps_col[0:1, 0:1])
        sr_ = r.op("scalar", "activation", rstd, scr, AF.Exp, scale=-0.5)
        r.wait("vector", "scalar", sr_)
        vn = r.op("vector", "scalar_tensor_tensor", scr, mean, -1.0, rstd,
                  op0=ALU.mult, op1=ALU.mult)
        r.wait("tensor", "vector", vn)
        self.bank_gate[self.pR].gate(r, "tensor")
        self.bank_gate[self.pN].gate(r, "tensor")
        r.mm(P[self.pR][:, 0:W], g.ones_row_f32[0:1, 0:128], rstd, start=True,
             stop=True)
        nm_mm = r.mm(P[self.pN][:, 0:W], g.ones_row_f32[0:1, 0:128], scr,
                     start=True, stop=True)
        r.wait("vector", "tensor", nm_mm)
        for et in range(CT):
            self.bg("lnxn").gate(r, "vector")
            r.op("vector", "tensor_tensor", self.xnew[:, 0:W],
                 tsb[:, et * tstride:et * tstride + W], P[self.pR][:, 0:W],
                 ALU.mult)
            v2 = r.op("vector", "tensor_tensor", self.xnew[:, 0:W],
                      self.xnew[:, 0:W], P[self.pN][:, 0:W], ALU.add)
            if tsb_uses is not None:
                tsb_uses(et, v2)
            self.bank_gate[self.pR].use("vector", v2)
            self.bank_gate[self.pN].use("vector", v2)
            if mout is None:
                self.sink(et, v2, self.xnew[:, 0:W], self)
            else:
                r.wait("sync", "vector", v2)
                wr = r.dma("sync", mout[et * 128:(et + 1) * 128, mcol:mcol + W],
                           self.xnew[:, 0:W],
                           allow_slow_non_contiguous=noncontig, slot="lnw")
                self.bg("lnxn").use_tok(wr)



def p2_l0_ffn(nc, r, ctx, g):
    min_, mout = g.p2_min, g.p2_mout
    P = g.psum

    NE = 8
    HT = 6
    HTS = 12

    xe = sbuf(ctx, nc, "p2_xe", (128, 8 * 512), BF16)
    w1 = sbuf(ctx, nc, "p2_w1", (128, 8 * 3072), BF16)
    w2 = sbuf(ctx, nc, "p2_w2", (128, 12 * 1024), BF16)
    gsb = sbuf(ctx, nc, "p2_gate", (128, 8 * 8), BF16)
    wd = sbuf(ctx, nc, "p2_wd", (128, 32 * 8), F32)
    usb = sbuf(ctx, nc, "p2_u", (128, 12 * 512), BF16)
    sg = sbuf(ctx, nc, "p2_sg", (128, 512), BF16)
    zacc = sbuf(ctx, nc, "p2_z", (128, 16 * 1024), BF16)
    gsc = sbuf(ctx, nc, "p2_gs", (128, 8 * 8), F32)
    xold = sbuf(ctx, nc, "p2_xold", (128, 8 * 512), F32)
    tsb = sbuf(ctx, nc, "p2_t", (128, 8 * 512), F32)
    bank_gate = [Gate() for _ in range(6)]
    tp_gate = [Gate(), Gate()]
    buf_gate = {}

    def bg(name):
        if name not in buf_gate:
            buf_gate[name] = Gate()
        return buf_gate[name]

    ln = LNKit(nc, r, ctx, g, bg, bank_gate, pS=2, pQ=3, pR=4, pN=5, name="p2ln")

    for c in range(CT):
        gw = r.dma("gpsimd", gsb[:, c * 8:(c + 1) * 8],
                   g.gate_wT[c * 128:(c + 1) * 128, :], slot="p2gw")

    for nt in range(8):
        bg("xe").gate(r, "gpsimd")
        for c in range(CT):
            xl = r.dma("gpsimd", xe[:, c * 512:(c + 1) * 512],
                       min_[c * 128:(c + 1) * 128,
                            MOFF + nt * 512:MOFF + (nt + 1) * 512], slot="p2xe")
        r.waitd("tensor", xl)
        if nt == 0:
            r.waitd("tensor", gw)
        for tt4 in range(4):
            tt = nt * 4 + tt4
            pb = tt4 % 2
            bank_gate[pb].gate(r, "tensor")
            for c in range(CT):
                mmc = r.mm(P[pb][:, 0:8],
                           xe[:, c * 512 + tt4 * 128:c * 512 + (tt4 + 1) * 128],
                           gsb[:, c * 8:(c + 1) * 8],
                           start=(c == 0), stop=(c == CT - 1))
                bg("xe").use("tensor", mmc)
            r.wait("vector", "tensor", mmc)
            S0 = gsc[:, 0:8]
            M1 = gsc[:, 8:9]
            NM = gsc[:, 9:10]
            SE = gsc[:, 10:11]
            RD = gsc[:, 11:12]
            M2 = gsc[:, 12:13]
            GE = gsc[:, 13:21]
            S2 = gsc[:, 21:29]
            r.op("vector", "tensor_reduce", M1, P[pb][:, 0:8], AX.X, ALU.max)
            bg("gNM").gate(r, "vector")
            vneg = r.op("vector", "tensor_scalar", NM, M1, -1.0, None, op0=ALU.mult)
            r.wait("scalar", "vector", vneg)
            bg("gS0").gate(r, "scalar")
            se_ = r.op("scalar", "activation", S0, P[pb][:, 0:8], AF.Exp,
                       bias=NM, accum_out=SE)
            bg("gNM").use("scalar", se_)
            bank_gate[pb].use("scalar", se_)
            bank_gate[pb].use("vector", vneg)
            r.wait("vector", "scalar", se_)
            r.op("vector", "reciprocal", RD, SE)
            r.op("vector", "tensor_scalar", S0, S0, RD, None, op0=ALU.mult)
            r.op("vector", "tensor_reduce", M1, S0, AX.X, ALU.max)
            r.op("vector", "tensor_scalar", GE, S0, M1, None, op0=ALU.is_ge)
            r.op("vector", "scalar_tensor_tensor", S2, GE, -1e9, S0,
                 op0=ALU.mult, op1=ALU.add)
            r.op("vector", "tensor_reduce", M2, S2, AX.X, ALU.max)
            r.op("vector", "tensor_scalar", GE, S0, M2, None, op0=ALU.is_ge)
            vw = r.op("vector", "tensor_tensor", wd[:, tt * 8:(tt + 1) * 8],
                      S0, GE, ALU.mult)
            bg("gS0").use("vector", vw)
            if g.dbg:
                r.wait("sync", "vector", vw)
                wr = r.dma("sync", g.dbg_wd[tt * 128:(tt + 1) * 128, :],
                           wd[:, tt * 8:(tt + 1) * 8], slot="p2dbg")
                bg("wd").use_tok(wr)
    wd_ready = r.cnt["vector"]

    def expert(e, half, shared):
        ht = HTS if shared else HT
        bg("w1").gate(r, "gpsimd")
        bg("w2").gate(r, "gpsimd")
        if shared:
            for c in range(CT):
                wld = r.dma("gpsimd", w1[:, c * 3072:(c + 1) * 3072],
                            g.Ws1T[c * 128:(c + 1) * 128, :], slot="p2w")
            for k in range(HTS):
                wld = r.dma("gpsimd", w2[:, k * 1024:(k + 1) * 1024],
                            g.Ws2T[k * 128:(k + 1) * 128, :], slot="p2w")
        else:
            for c in range(CT):
                wld = r.dma("gpsimd", w1[:, c * 3072:c * 3072 + 1536],
                            g.W1T[e:e + 1, c * 128:(c + 1) * 128, :]
                            .rearrange("o p f -> (o p) f"), slot="p2w")
            for k in range(HT):
                wld = r.dma("gpsimd", w2[:, k * 1024:(k + 1) * 1024],
                            g.W2T[e:e + 1, k * 128:(k + 1) * 128, :]
                            .rearrange("o p f -> (o p) f"), slot="p2w")

        for nt4 in range(4):
            nt = half * 4 + nt4
            bg("xe").gate(r, "gpsimd")
            for c in range(CT):
                xl = r.dma("gpsimd", xe[:, c * 512:(c + 1) * 512],
                           min_[c * 128:(c + 1) * 128,
                                MOFF + nt * 512:MOFF + (nt + 1) * 512], slot="p2xe")
            r.waitd("tensor", xl)
            r.waitd("tensor", wld)
            for m in range(ht):
                pbg_ = m % 2
                pby_ = 2 + m % 2
                bank_gate[pbg_].gate(r, "tensor")
                for c in range(CT):
                    gm = r.mm(P[pbg_][:, :],
                              w1[:, c * 3072 + (ht + m) * 128:
                                 c * 3072 + (ht + m + 1) * 128],
                              xe[:, c * 512:(c + 1) * 512],
                              start=(c == 0), stop=(c == CT - 1))
                    bg("xe").use("tensor", gm)
                    bg("w1").use("tensor", gm)
                r.wait("scalar", "tensor", gm)
                bg("sg").gate(r, "scalar")
                ssg = r.op("scalar", "activation", sg[:, :], P[pbg_][:, :], AF.Silu)
                bank_gate[pbg_].use("scalar", ssg)
                bank_gate[pby_].gate(r, "tensor")
                for c in range(CT):
                    ym = r.mm(P[pby_][:, :],
                              w1[:, c * 3072 + m * 128:c * 3072 + (m + 1) * 128],
                              xe[:, c * 512:(c + 1) * 512],
                              start=(c == 0), stop=(c == CT - 1))
                    bg("xe").use("tensor", ym)
                    bg("w1").use("tensor", ym)
                r.wait("vector", "tensor", ym)
                r.wait("vector", "scalar", ssg)
                bg(("usb", m)).gate(r, "vector")
                vu = r.op("vector", "tensor_tensor", usb[:, m * 512:(m + 1) * 512],
                          P[pby_][:, :], sg[:, :], ALU.mult)
                bank_gate[pby_].use("vector", vu)
                bg("sg").use("vector", vu)
            udone = r.cnt["vector"]
            r.wait("tensor", "vector", udone)
            if nt4 == 0 and half == 0 and e == 0 and not shared:
                r.wait("vector", "vector", wd_ready)
            for tt4 in range(4):
                tt = nt * 4 + tt4
                for n2 in range(2):
                    pb = 4 + n2
                    bank_gate[pb].gate(r, "tensor")
                    for mk in range(ht):
                        em = r.mm(P[pb][:, :],
                                  usb[:, mk * 512 + tt4 * 128:
                                      mk * 512 + (tt4 + 1) * 128],
                                  w2[:, mk * 1024 + n2 * 512:
                                     mk * 1024 + (n2 + 1) * 512],
                                  start=(mk == 0), stop=(mk == ht - 1))
                        bg(("usb", mk)).use("tensor", em)
                        bg("w2").use("tensor", em)
                    r.wait("vector", "tensor", em)
                    zsl = zacc[:, (nt4 * 4 + tt4) * 1024 + n2 * 512:
                               (nt4 * 4 + tt4) * 1024 + (n2 + 1) * 512]
                    if shared:
                        sc1 = 1.0
                    else:
                        sc1 = wd[:, tt * 8 + e:tt * 8 + e + 1]
                    if e == 0 and not shared:
                        vz = r.op("vector", "tensor_scalar", zsl, P[pb][:, :],
                                  sc1, None, op0=ALU.mult)
                    else:
                        vz = r.op("vector", "scalar_tensor_tensor", zsl, P[pb][:, :],
                                  sc1, zsl, op0=ALU.mult, op1=ALU.add)
                    bank_gate[pb].use("vector", vz)

    for half in range(2):
        bg("zacc").gate(r, "vector")
        for e in range(NE):
            expert(e, half, shared=False)
        expert(0, half, shared=True)
        zdone = r.cnt["vector"]
        if g.dbg:
            r.wait("sync", "vector", zdone)
            for tt16 in range(16):
                wr = r.dma("sync",
                           g.dbg_z[half * 2048 + tt16 * 128:
                                   half * 2048 + (tt16 + 1) * 128, :],
                           zacc[:, tt16 * 1024:(tt16 + 1) * 1024], slot="p2dbg")
                bg("zacc").use_tok(wr)

        for nt4 in range(4):
            nt = half * 4 + nt4
            bg("xold").gate(r, "sync")
            for c in range(CT):
                xo = r.dma("sync", xold[:, c * 512:(c + 1) * 512],
                           min_[c * 128:(c + 1) * 128,
                                MOFF + nt * 512:MOFF + (nt + 1) * 512],
                           slot="p2xold")
            r.waitd("vector", xo)
            tvs = []
            for c in range(CT):
                tpb = g.tpb[c % 2]
                tp_gate[c % 2].gate(r, "tensor")
                r.wait("tensor", "vector", zdone)
                for tt4 in range(4):
                    zin = zacc[:, (nt4 * 4 + tt4) * 1024 + c * 128:
                               (nt4 * 4 + tt4) * 1024 + (c + 1) * 128]
                    tc = r.op("tensor", "transpose", tpb[:, tt4 * 128:(tt4 + 1) * 128],
                              zin, g.ident_bf[:, :])
                    bg("zacc").use("tensor", tc)
                r.wait("vector", "tensor", tc)
                tv = r.op("vector", "tensor_tensor", tsb[:, c * 512:(c + 1) * 512],
                          xold[:, c * 512:(c + 1) * 512], tpb[:, 0:512], ALU.add)
                tp_gate[c % 2].use("vector", tv)
                bg("xold").use("vector", tv)
                tvs.append(tv)
                ln.stats_et(tsb[:, c * 512:(c + 1) * 512], c, tv)

            def tsb_uses(et, v2):
                pass
            ln.finish(tsb, mout, MOFF + nt * 512)
    zs = sbuf(ctx, nc, "p2_zero", (128, 64), F32)
    mz = r.op("gpsimd", "memset", zs[:, :], 0.0)
    r.wait("sync", "gpsimd", mz)
    for c in range(CT):
        r.dma("sync", mout[c * 128:(c + 1) * 128, 0:MOFF], zs[:, 0:MOFF],
              slot="p2mz")
        r.dma("sync", mout[c * 128:(c + 1) * 128, MCOLS - 1:MCOLS], zs[:, 0:1],
              allow_slow_non_contiguous=True, slot="p2mz")
    r.flush_dmas()
    fin = {e_: r.cnt[e_] for e_ in ENGINES}
    for e_ in ENGINES:
        for o in ENGINES:
            if o != e_:
                r.wait(e_, o, fin[o])




def p3_l1_attn(nc, r, ctx, g):
    min_, mout = g.p3_min, g.p3_mout
    P = g.psum

    wqkv = sbuf(ctx, nc, "p3_wqkv", (128, 8 * 3072), BF16)
    wout = sbuf(ctx, nc, "p3_wout", (128, 8 * 1024), BF16)
    kron = sbuf(ctx, nc, "p3_kron", (128, NH * 4 * 512), BF16)
    bout = sbuf(ctx, nc, "p3_bout", (128, 8), F32)
    xsl = [sbuf(ctx, nc, f"p3_xs{i}", (128, 8 * 288), BF16) for i in range(2)]
    ktq = sbuf(ctx, nc, "p3_ktq", (64, NH * 160), BF16)
    qts = sbuf(ctx, nc, "p3_qts", (64, NH * 128), BF16)
    vsh = [sbuf(ctx, nc, f"p3_vsh{i}", (128, C), BF16) for i in range(3)]
    smf = sbuf(ctx, nc, "p3_smf", (128, 176), F32)
    dsum = sbuf(ctx, nc, "p3_dsum", (128, NH), F32)
    drec = sbuf(ctx, nc, "p3_drec", (128, NH), F32)
    eq = sbuf(ctx, nc, "p3_eq", (128, NH * 176), BF16)
    fsb = sbuf(ctx, nc, "p3_f", (128, 512), BF16)
    fsbB = sbuf(ctx, nc, "p3_fB", (32, 512), BF16)
    fsbb = sbuf(ctx, nc, "p3_fbos", (1, 512), BF16)
    osb = sbuf(ctx, nc, "p3_o", (128, 8 * 128), BF16)
    vbos = sbuf(ctx, nc, "p3_vbos", (1, C), BF16)
    vbosT = sbuf(ctx, nc, "p3_vbosT", (128, 8), BF16)
    kbosT = sbuf(ctx, nc, "p3_kbosT", (64, NH), BF16)
    xold = sbuf(ctx, nc, "p3_xold", (128, 8 * 128), F32)
    tsb = sbuf(ctx, nc, "p3_t", (128, 8 * 128), F32)
    xold0 = sbuf(ctx, nc, "p3_xold0", (128, 8), F32)
    t0sb = sbuf(ctx, nc, "p3_t0", (128, 8), F32)
    zstub = sbuf(ctx, nc, "p3_zero", (128, 64), F32)

    bank_gate = [Gate() for _ in range(6)]
    buf_gate = {}

    def bg(name):
        if name not in buf_gate:
            buf_gate[name] = Gate()
        return buf_gate[name]

    ln = LNKit(nc, r, ctx, g, bg, bank_gate, pS=0, pQ=1, pR=2, pN=3, W=128,
               name="p3ln")

    r.op("gpsimd", "memset", zstub[:, :], 0.0)
    for c in range(CT):
        wload = r.dma("gpsimd", wqkv[:, c * 3072:(c + 1) * 3072],
                      g.w_qkv[c * 128:(c + 1) * 128, :], slot="p3w")
        wload = r.dma("gpsimd", wout[:, c * 1024:(c + 1) * 1024],
                      g.w_out[c * 128:(c + 1) * 128, :], slot="p3w")
    for b4 in range(8):
        wload = r.dma("gpsimd", kron[:, b4 * 4096:(b4 + 1) * 4096],
                      g.wth_kron[:, b4 * 4096:(b4 + 1) * 4096], slot="p3w")
    wload = r.dma("gpsimd", bout[:, :], g.b_outT, slot="p3w")

    r.wait("sync", "gpsimd", 1)
    for c in range(CT):
        r.dma("sync", mout[c * 128:(c + 1) * 128, 0:MOFF], zstub[:, 0:MOFF],
              slot="p3mz")
        r.dma("sync", mout[c * 128:(c + 1) * 128, MCOLS - 1:MCOLS], zstub[:, 0:1],
              allow_slow_non_contiguous=True, slot="p3mz")

    xb0 = xsl[0]
    for c in range(CT):
        xlt = r.dma("gpsimd", xb0[:, c * 288:c * 288 + 288],
                    min_[c * 128:(c + 1) * 128, 0:288], slot="p3x0")
    r.waitd("tensor", xlt)
    r.waitd("tensor", wload)
    for n in range(2):
        pb = 4 + n
        bank_gate[pb].gate(r, "tensor")
        for c in range(CT):
            mmc = r.mm(P[pb][0:1, 0:512],
                       xb0[:, c * 288 + 31:c * 288 + 32],
                       wqkv[:, c * 3072 + 2048 + n * 512:
                            c * 3072 + 2048 + (n + 1) * 512],
                       start=(c == 0), stop=(c == CT - 1))
        r.wait("vector", "tensor", mmc)
        vb = r.op("vector", "tensor_copy", vbos[0:1, n * 512:(n + 1) * 512],
                  P[pb][0:1, 0:512])
        bank_gate[pb].use("vector", vb)
    tp_gate3 = [Gate(), Gate()]
    for c in range(CT):
        tpb = g.tpb[c % 2]
        tp_gate3[c % 2].gate(r, "tensor")
        tc = r.op("tensor", "transpose", tpb[0:128, 0:1],
                  vbos[0:1, c * 128:(c + 1) * 128], g.ident_bf[0:1, 0:1])
        bg("vbos").use("tensor", tc)
        r.wait("vector", "tensor", tc)
        cc2 = r.op("vector", "tensor_copy", vbosT[:, c:c + 1], tpb[0:128, 0:1])
        tp_gate3[c % 2].use("vector", cc2)
    for f in range(CT):
        pb = 4 + f % 2
        bank_gate[pb].gate(r, "tensor")
        for c in range(CT):
            mmc = r.mm(P[pb][0:128, 0:1],
                       wqkv[:, c * 3072 + 1024 + f * 128:
                            c * 3072 + 1024 + (f + 1) * 128],
                       xb0[:, c * 288 + 31:c * 288 + 32],
                       start=(c == 0), stop=(c == CT - 1))
        r.wait("vector", "tensor", mmc)
        c0 = r.op("vector", "tensor_copy", kbosT[0:64, 2 * f:2 * f + 1],
                  P[pb][0:64, 0:1])
        c1 = r.op("vector", "tensor_copy", kbosT[0:64, 2 * f + 1:2 * f + 2],
                  P[pb][64:128, 0:1])
        bank_gate[pb].use("vector", c1)

    def make_vsh(t, xbuf, xbase):
        w = min(128, MCOLS - 128 * t)
        sl = t % 3
        bg(("vsh", sl)).gate(r, "vector")
        for n in range(2):
            pb = 4 + n
            bank_gate[pb].gate(r, "tensor")
            for c in range(CT):
                mmc = r.mm(P[pb][0:w, :],
                           xbuf[:, c * 288 + 128 * t - xbase:
                                c * 288 + 128 * t - xbase + w],
                           wqkv[:, c * 3072 + 2048 + n * 512:
                                c * 3072 + 2048 + (n + 1) * 512],
                           start=(c == 0), stop=(c == CT - 1))
                bg(("xsl", 0 if xbuf is xsl[0] else 1)).use("tensor", mmc)
            r.wait("vector", "tensor", mmc)
            vc = r.op("vector", "tensor_copy", vsh[sl][0:w, n * 512:(n + 1) * 512],
                      P[pb][0:w, :])
            bank_gate[pb].use("vector", vc)
        return r.cnt["vector"]

    vsh_ready = {}
    vsh_ready[0] = make_vsh(0, xb0, 0)
    vsh_ready[1] = make_vsh(1, xb0, 0)

    for qt in range(32):
        K0 = 128 * qt
        xbuf = xsl[qt % 2]
        span = min(288, MCOLS - K0)
        if qt > 0:
            bg(("xsl", qt % 2)).gate(r, "gpsimd")
            for c in range(CT):
                xlt = r.dma("gpsimd", xbuf[:, c * 288:c * 288 + span],
                            min_[c * 128:(c + 1) * 128, K0:K0 + span],
                            slot=f"p3x{qt % 2}")
            r.waitd("tensor", xlt)
        vsh_ready[qt + 1] = make_vsh(qt + 1, xbuf, K0)

        bg("ktq").gate(r, "vector")
        for f in range(CT):
            pb = 4 + f % 2
            bank_gate[pb].gate(r, "tensor")
            for c in range(CT):
                mmc = r.mm(P[pb][:, 0:160],
                           wqkv[:, c * 3072 + 1024 + f * 128:
                                c * 3072 + 1024 + (f + 1) * 128],
                           xbuf[:, c * 288:c * 288 + 160],
                           start=(c == 0), stop=(c == CT - 1))
                bg(("xsl", qt % 2)).use("tensor", mmc)
            r.wait("vector", "tensor", mmc)
            c0 = r.op("vector", "tensor_copy",
                      ktq[0:64, (2 * f) * 160:(2 * f + 1) * 160], P[pb][0:64, 0:160])
            c1 = r.op("vector", "tensor_copy",
                      ktq[0:64, (2 * f + 1) * 160:(2 * f + 2) * 160],
                      P[pb][64:128, 0:160])
            bank_gate[pb].use("vector", c1)
        bg("qts").gate(r, "scalar")
        for f in range(CT):
            pb = 4 + f % 2
            bank_gate[pb].gate(r, "tensor")
            for c in range(CT):
                mmc = r.mm(P[pb][:, 0:128],
                           wqkv[:, c * 3072 + f * 128:c * 3072 + (f + 1) * 128],
                           xbuf[:, c * 288 + 32:c * 288 + 160],
                           start=(c == 0), stop=(c == CT - 1))
                bg(("xsl", qt % 2)).use("tensor", mmc)
            r.wait("scalar", "tensor", mmc)
            s0 = r.op("scalar", "activation", qts[0:64, (2 * f) * 128:(2 * f + 1) * 128],
                      P[pb][0:64, 0:128], AF.Copy, scale=0.125)
            s1 = r.op("scalar", "activation",
                      qts[0:64, (2 * f + 1) * 128:(2 * f + 2) * 128],
                      P[pb][64:128, 0:128], AF.Copy, scale=0.125)
            bank_gate[pb].use("scalar", s1)
        qts_ready = r.cnt["scalar"]
        ktq_ready = r.cnt["vector"]

        am_off = 176 if qt == 0 else 0
        slA = qt % 3
        slB = (qt + 1) % 3

        for hh in range(NH):
            pb = hh % 2
            bank_gate[pb].gate(r, "tensor")
            if hh == 0:
                r.wait("tensor", "scalar", qts_ready)
                r.wait("tensor", "vector", ktq_ready)
            r.mm(P[pb][:, 0:160], qts[0:64, hh * 128:(hh + 1) * 128],
                 ktq[0:64, hh * 160:(hh + 1) * 160], start=True, stop=False)
            smm = r.mm(P[pb][:, 160:161], qts[0:64, hh * 128:(hh + 1) * 128],
                       kbosT[0:64, hh:hh + 1], start=True, stop=True)
            bg("ktq").use("tensor", smm)
            bg("qts").use("tensor", smm)
            bg("kbosT").use("tensor", smm)
            r.wait("vector", "tensor", smm)
            bg("smf").gate(r, "vector")
            vm = r.op("vector", "tensor_tensor", smf[:, 0:161], P[pb][:, 0:161],
                      g.l1_amask[:, am_off:am_off + 161], ALU.add)
            bank_gate[pb].use("vector", vm)
            r.wait("scalar", "vector", vm)
            bg(("eqh", hh)).gate(r, "scalar")
            se_ = r.op("scalar", "activation", eq[:, hh * 176:hh * 176 + 161],
                       smf[:, 0:161], AF.Exp, accum_out=dsum[:, hh:hh + 1])
            bg("smf").use("scalar", se_)
            r.wait("vector", "scalar", se_)
            vr = r.op("vector", "reciprocal", drec[:, hh:hh + 1], dsum[:, hh:hh + 1])
            vn = r.op("vector", "tensor_scalar", eq[:, hh * 176:hh * 176 + 161],
                      eq[:, hh * 176:hh * 176 + 161], drec[:, hh:hh + 1], None,
                      op0=ALU.mult)
            bg(("eqh", hh)).use("vector", vn)
        eq_ready = r.cnt["vector"]

        r.wait("tensor", "vector", eq_ready)
        for gb in range(4):
            pfa = 2 + gb % 2
            pfb = 4 + gb % 2
            bank_gate[pfa].gate(r, "tensor")
            bank_gate[pfb].gate(r, "tensor")
            for hh in range(NH):
                rh = kron[:, (hh * 4 + gb) * 512:(hh * 4 + gb + 1) * 512]
                fa = r.mm(P[pfa][:, :], eq[:, hh * 176:hh * 176 + 128], rh,
                          start=(hh == 0), stop=False)
                fb = r.mm(P[pfb][0:33, :], eq[:, hh * 176 + 128:hh * 176 + 161], rh,
                          start=(hh == 0), stop=(hh == NH - 1))
                bg(("eqh", hh)).use("tensor", fb)
                bg("kron").use("tensor", fb)
            r.wait("vector", "tensor", fb)
            bg("fsb").gate(r, "vector")
            e1 = r.op("vector", "tensor_copy", fsb[:, :], P[pfa][:, :])
            e2 = r.op("vector", "tensor_copy", fsbB[:, :], P[pfb][0:32, :])
            e2 = r.op("vector", "tensor_copy", fsbb[:, :], P[pfb][32:33, :])
            bank_gate[pfa].use("vector", e1)
            bank_gate[pfb].use("vector", e2)
            r.wait("tensor", "vector", e2)
            for gi in range(4):
                gg = gb * 4 + gi
                h2 = gg % 2
                pu = gg // 2 % 2
                if h2 == 0:
                    bank_gate[pu].gate(r, "tensor")
                u1 = r.mm(P[pu][h2 * 64:(h2 + 1) * 64, 0:128],
                          vsh[slA][:, gg * 64:(gg + 1) * 64],
                          fsb[:, gi * 128:(gi + 1) * 128], start=True, stop=False)
                u2 = r.mm(P[pu][h2 * 64:(h2 + 1) * 64, 0:128],
                          vsh[slB][0:32, gg * 64:(gg + 1) * 64],
                          fsbB[0:32, gi * 128:(gi + 1) * 128],
                          start=False, stop=False)
                u3 = r.mm(P[pu][h2 * 64:(h2 + 1) * 64, 0:128],
                          vbos[0:1, gg * 64:(gg + 1) * 64],
                          fsbb[0:1, gi * 128:(gi + 1) * 128],
                          start=False, stop=True)
                bg("fsb").use("tensor", u3)
                bg(("vsh", slA)).use("tensor", u3)
                bg(("vsh", slB)).use("tensor", u3)
                bg("vbos").use("tensor", u3)
                if h2 == 1:
                    hp = gg // 2
                    r.wait("vector", "tensor", u3)
                    bg("osb").gate(r, "vector")
                    vo = r.op("vector", "tensor_copy",
                              osb[:, hp * 128:(hp + 1) * 128], P[pu][:, 0:128])
                    bank_gate[pu].use("vector", vo)
        o_ready = r.cnt["vector"]
        if g.dbg:
            r.wait("sync", "vector", o_ready)
            for hp in range(8):
                wr = r.dma("sync", g.dbg3_o[hp * 128:(hp + 1) * 128,
                                            qt * 128:(qt + 1) * 128],
                           osb[:, hp * 128:(hp + 1) * 128], slot="p3dbg")
                bg("osb").use_tok(wr)

        W_LN = 128 if qt < 31 else 127
        bg("xold").gate(r, "sync")
        for c in range(CT):
            xold_ready = r.dma("sync", xold[:, c * 128:c * 128 + W_LN],
                               min_[c * 128:(c + 1) * 128,
                                    MOFF + K0 + 1:MOFF + K0 + 1 + W_LN],
                               slot="p3xold")

        r.wait("tensor", "vector", o_ready)
        r.waitd("vector", xold_ready)
        for et in range(CT):
            pb = 4 + et % 2
            bank_gate[pb].gate(r, "tensor")
            for c in range(CT):
                mmc = r.mm(P[pb][:, 0:128],
                           wout[:, c * 1024 + et * 128:c * 1024 + (et + 1) * 128],
                           osb[:, c * 128:(c + 1) * 128],
                           start=(c == 0), stop=(c == CT - 1))
                bg("osb").use("tensor", mmc)
            r.wait("vector", "tensor", mmc)
            bg(("tsb", et)).gate(r, "vector")
            tv = r.op("vector", "scalar_tensor_tensor",
                      tsb[:, et * 128:et * 128 + W_LN], P[pb][:, 0:W_LN],
                      bout[:, et:et + 1], xold[:, et * 128:et * 128 + W_LN],
                      op0=ALU.add, op1=ALU.add)
            bank_gate[pb].use("vector", tv)
            bg("xold").use("vector", tv)
            ln.stats_et(tsb[:, et * 128:et * 128 + W_LN], et, tv, W=W_LN)

        def uses(et, v2):
            bg(("tsb", et)).use("vector", v2)
        ln.finish(tsb, mout, MOFF + K0 + 1, tsb_uses=uses, W=W_LN, tstride=128)

    for c in range(CT):
        x0_ready = r.dma("sync", xold0[:, c:c + 1],
                         min_[c * 128:(c + 1) * 128, MOFF:MOFF + 1],
                         allow_slow_non_contiguous=True, slot="p3xold")
    bank_gate[4].gate(r, "tensor")
    for et in range(CT):
        for c in range(CT):
            mmc = r.mm(P[4][:, et:et + 1],
                       wout[:, c * 1024 + et * 128:c * 1024 + (et + 1) * 128],
                       vbosT[:, c:c + 1],
                       start=(c == 0 and et == 0), stop=False)
            bg("vbosT").use("tensor", mmc)
    r.wait("vector", "tensor", r.cnt["tensor"])
    r.waitd("vector", x0_ready)
    for et in range(CT):
        tv0 = r.op("vector", "scalar_tensor_tensor", t0sb[:, et:et + 1],
                   P[4][:, et:et + 1], bout[:, et:et + 1], xold0[:, et:et + 1],
                   op0=ALU.add, op1=ALU.add)
        ln.stats_et(t0sb[:, et:et + 1], et, tv0, W=1)
    bank_gate[4].use("vector", r.cnt["vector"])
    ln.finish(t0sb, mout, MOFF, W=1, tstride=1, noncontig=True)

    r.flush_dmas()
    fin = {e_: r.cnt[e_] for e_ in ENGINES}
    for e_ in ENGINES:
        for o in ENGINES:
            if o != e_:
                r.wait(e_, o, fin[o])




def p4_l1_ffn(nc, r, ctx, g):
    min_ = g.p4_min
    out = g.out
    P = g.psum
    NP = 2048

    xe = sbuf(ctx, nc, "p4_xe", (128, 8 * 512), BF16)
    etile = sbuf(ctx, nc, "p4_e", (128, NP), BF16)
    dtile = sbuf(ctx, nc, "p4_d", (128, NP), BF16)
    snp = sbuf(ctx, nc, "p4_snp", (128, 4), F32)
    snr = sbuf(ctx, nc, "p4_snr", (128, 32), F32)
    bT = sbuf(ctx, nc, "p4_bT", (128, 8 * 64), F32)
    slotsT = sbuf(ctx, nc, "p4_slotsT", (128, 8 * NP), BF16)
    ysb = sbuf(ctx, nc, "p4_y", (128, 16 * 1024), BF16)

    bank_gate = [Gate() for _ in range(6)]
    tp_gate4 = [Gate(), Gate()]
    buf_gate = {}

    def bg(name):
        if name not in buf_gate:
            buf_gate[name] = Gate()
        return buf_gate[name]

    ln = LNKit(nc, r, ctx, g, bg, bank_gate, pS=0, pQ=1, pR=2, pN=3, W=512,
               name="p4ln")

    r.dma("gpsimd", bT[:, :], g.bT_d, slot="p4w")

    with ExitStack() as s1:
        phi = sbuf(s1, nc, "p4_phi", (128, 8 * NP), BF16)
        etf = sbuf(s1, nc, "p4_ef", (128, NP), F32)
        for c in range(CT):
            wload = r.dma("gpsimd", phi[:, c * NP:(c + 1) * NP],
                          g.phi_d[c * 128:(c + 1) * 128, :], slot="p4w")

        for mt in range(32):
            m0 = mt * 128
            sub = mt % 4
            if sub == 0:
                bg("xe").gate(r, "gpsimd")
                for c in range(CT):
                    xlt = r.dma("gpsimd", xe[:, c * 512:(c + 1) * 512],
                                min_[c * 128:(c + 1) * 128,
                                     MOFF + m0:MOFF + m0 + 512], slot="p4xe")
                r.waitd("tensor", xlt)
                if mt == 0:
                    r.waitd("tensor", wload)
            bg("etf").gate(r, "scalar")
            for n4 in range(4):
                pb = n4 % 2
                bank_gate[pb].gate(r, "tensor")
                for c in range(CT):
                    mmc = r.mm(P[pb][:, :],
                               xe[:, c * 512 + sub * 128:c * 512 + (sub + 1) * 128],
                               phi[:, c * NP + n4 * 512:c * NP + (n4 + 1) * 512],
                               start=(c == 0), stop=(c == CT - 1))
                    bg("xe").use("tensor", mmc)
                    bg("phi").use("tensor", mmc)
                r.wait("scalar", "tensor", mmc)
                if n4 == 0:
                    bg("snp").gate(r, "scalar")
                se_ = r.op("scalar", "activation", etf[:, n4 * 512:(n4 + 1) * 512],
                           P[pb][:, :], AF.Exp, accum_out=snp[:, n4:n4 + 1])
                bank_gate[pb].use("scalar", se_)
            r.wait("vector", "scalar", se_)
            vv = r.op("vector", "tensor_reduce", snr[:, mt:mt + 1], snp[:, :],
                      AX.X, ALU.add)
            bg("snp").use("vector", vv)
            r.wait("sync", "scalar", se_)
            ewr_done = r.dma("sync", g.Eb[m0:m0 + 128, :], etf[:, :], slot="p4ew")
            bg("etf").use_tok(ewr_done)
        if g.dbg and (g.dbgmask & 1):
            r.dma("sync", g.dbg4_E, g.Eb[:, :], slot="p4dbg")
        vs = r.op("vector", "reciprocal", snr[:, :], snr[:, :])
        snr_done = vs
        r.join()

    r.waitd("gpsimd", ewr_done)
    cc = r.op("gpsimd", "collective_compute", "AllReduce", ALU.add,
              replica_groups=[list(range(8))],
              ins=[g.Eb_t.ap()], outs=[g.Db_t.ap()], sem="cc")

    if g.dbg and (g.dbgmask & 2):
        r.wait("sync", "cc", cc)
        r.dma("sync", g.dbg4_D, g.Db[:, :], slot="p4dbg")

    with ExitStack() as s2:
        dtf = sbuf(s2, nc, "p4_dtf", (128, NP), F32)
        disp = [sbuf(s2, nc, f"p4_disp{i}", (128, NP), BF16) for i in range(4)]
        xtok = sbuf(s2, nc, "p4_xtok", (128, 4 * 1024), BF16)
        slots = sbuf(s2, nc, "p4_slots", (128, 16 * 1024), F32)

        slots_init = False
        for ch in range(8):
            bg("xe").gate(r, "gpsimd")
            for c in range(CT):
                xlt = r.dma("gpsimd", xe[:, c * 512:(c + 1) * 512],
                            min_[c * 128:(c + 1) * 128,
                                 MOFF + ch * 512:MOFF + ch * 512 + 512],
                            slot="p4xe")
            for i4 in range(4):
                mt = ch * 4 + i4
                m0 = mt * 128
                bg("etile").gate(r, "gpsimd")
                le = r.dma("gpsimd", etile[:, :], g.Eb[m0:m0 + 128, :], slot="p4et")
                bg("dtile").gate(r, "gpsimd")
                if mt == 0:
                    r.wait("gpsimd", "cc", cc)
                ld = r.dma("gpsimd", dtile[:, :], g.Db[m0:m0 + 128, :], slot="p4dt")
                r.waitd("vector", le)
                r.waitd("vector", ld)
                bg(("disp", i4)).gate(r, "vector")
                bg("dtf").gate(r, "vector")
                r.op("vector", "reciprocal", dtf[:, :], dtile[:, :])
                vd = r.op("vector", "tensor_tensor", disp[i4][:, :], etile[:, :],
                          dtf[:, :], ALU.mult)
                bg("etile").use("vector", vd)
                bg("dtile").use("vector", vd)
                bg("dtf").use("vector", vd)
                bg(("xtok", i4)).gate(r, "vector")
                r.waitd("tensor", xlt)
                for c in range(CT):
                    tpb = g.tpb[c % 2]
                    tp_gate4[c % 2].gate(r, "tensor")
                    tc = r.op("tensor", "transpose", tpb[:, 0:128],
                              xe[:, c * 512 + i4 * 128:c * 512 + (i4 + 1) * 128],
                              g.ident_bf[:, :])
                    bg("xe").use("tensor", tc)
                    r.wait("vector", "tensor", tc)
                    vc = r.op("vector", "tensor_copy",
                              xtok[:, i4 * 1024 + c * 128:i4 * 1024 + (c + 1) * 128],
                              tpb[:, 0:128])
                    tp_gate4[c % 2].use("vector", vc)
            xt_done = r.cnt["vector"]
            if g.dbg and (g.dbgmask & 4):
                r.wait("sync", "vector", xt_done)
                for i4 in range(4):
                    if ch in (0, 7):
                        half = 0 if ch == 0 else 256
                        if i4 < 2:
                            dtk = r.dma("sync",
                                        g.dbg4_dp[half + i4 * 128:
                                                  half + (i4 + 1) * 128, :],
                                        disp[i4][:, :], slot="p4dbg")
                            bg(("disp", i4)).use_tok(dtk)
                    wr = r.dma("sync",
                               g.dbg4_xt[ch * 512 + i4 * 128:
                                         ch * 512 + (i4 + 1) * 128, :],
                               xtok[:, i4 * 1024:(i4 + 1) * 1024], slot="p4dbg")
                    bg(("xtok", i4)).use_tok(wr)
            r.wait("tensor", "vector", xt_done)
            for j in range(16):
                for n2 in range(2):
                    pb = 4 + n2
                    bank_gate[pb].gate(r, "tensor")
                    for i4 in range(4):
                        mmc = r.mm(P[pb][:, :],
                                   disp[i4][:, j * 128:(j + 1) * 128],
                                   xtok[:, i4 * 1024 + n2 * 512:
                                        i4 * 1024 + (n2 + 1) * 512],
                                   start=(i4 == 0), stop=(i4 == 3))
                        bg(("disp", i4)).use("tensor", mmc)
                        bg(("xtok", i4)).use("tensor", mmc)
                    r.wait("vector", "tensor", mmc)
                    sl = slots[:, j * 1024 + n2 * 512:j * 1024 + (n2 + 1) * 512]
                    if not slots_init:
                        va = r.op("vector", "tensor_copy", sl, P[pb][:, :])
                    else:
                        va = r.op("vector", "tensor_tensor", sl, sl, P[pb][:, :],
                                  ALU.add)
                    bank_gate[pb].use("vector", va)
            slots_init = True

        r.join()
        if g.dbg and (g.dbgmask & 8):
            r.wait("sync", "vector", r.cnt["vector"])
            r.dma("sync", g.dbg4_id, g.ident_bf[:, :], slot="p4dbg")
            for j in range(16):
                wr = r.dma("sync", g.dbg4_sl[j * 128:(j + 1) * 128, :],
                           slots[:, j * 1024:(j + 1) * 1024], slot="p4dbg")
                bg("slots_d").use_tok(wr)
            r.join()
        with ExitStack() as s2b:
            slotsb = sbuf(s2b, nc, "p4_slotsb", (128, 1024), BF16)
            for j in range(16):
                bg("slotsb").gate(r, "scalar")
                cv = r.op("scalar", "activation", slotsb[:, :],
                          slots[:, j * 1024:(j + 1) * 1024], AF.Copy)
                r.wait("tensor", "scalar", cv)
                for c in range(CT):
                    tpb = g.tpb[c % 2]
                    tp_gate4[c % 2].gate(r, "tensor")
                    tc = r.op("tensor", "transpose", tpb[:, 0:128],
                              slotsb[:, c * 128:(c + 1) * 128], g.ident_bf[:, :])
                    bg("slotsb").use("tensor", tc)
                    r.wait("vector", "tensor", tc)
                    vc = r.op("vector", "tensor_copy",
                              slotsT[:, c * NP + j * 128:c * NP + (j + 1) * 128],
                              tpb[:, 0:128])
                    tp_gate4[c % 2].use("vector", vc)
        slT_done = r.cnt["vector"]
        r.join()

    with ExitStack() as s3:
        wn = [sbuf(s3, nc, f"p4_wn{i}", (128, 8 * 1024), BF16) for i in range(3)]
        yT = sbuf(s3, nc, "p4_yT", (128, 8 * NP), BF16)

        def load_wn(n, buf):
            bg(("wn", buf)).gate(r, "gpsimd")
            for c in range(CT):
                tok = r.dma("gpsimd", wn[buf][:, c * 1024:(c + 1) * 1024],
                            g.Wl1_d[n:n + 1, c * 128:(c + 1) * 128, :]
                            .rearrange("o p f -> (o p) f"), slot=f"p4wn{buf}")
            return tok

        wn_ld = {0: load_wn(0, 0), 1: load_wn(1, 1)}
        for n in range(64):
            buf = n % 3
            if n + 2 < 64:
                wn_ld[n + 2] = load_wn(n + 2, (n + 2) % 3)
            r.waitd("tensor", wn_ld[n])
            for et in range(CT):
                pb = 4 + et % 2
                bank_gate[pb].gate(r, "tensor")
                reg = P[pb][:, 0:32]
                for c in range(CT):
                    mmc = r.mm(reg,
                               wn[buf][:, c * 1024 + et * 128:
                                       c * 1024 + (et + 1) * 128],
                               slotsT[:, c * NP + n * 32:c * NP + (n + 1) * 32],
                               start=(c == 0), stop=(c == CT - 1))
                    bg(("wn", buf)).use("tensor", mmc)
                    bg("slotsT").use("tensor", mmc)
                r.wait("vector", "tensor", mmc)
                vy = r.op("vector", "tensor_scalar",
                          yT[:, et * NP + n * 32:et * NP + (n + 1) * 32], reg,
                          bT[:, et * 64 + n:et * 64 + n + 1], None, op0=ALU.add)
                bank_gate[pb].use("vector", vy)
            if g.dbg and (g.dbgmask & 16) and n == 10:
                r.wait("sync", "tensor", r.cnt["tensor"])
                wr = r.dma("sync", g.dbg4_wn, wn[buf][:, :], slot="p4dbg")
                bg(("wn", buf)).use_tok(wr)
        yT_done = r.cnt["vector"]
        if g.dbg and (g.dbgmask & 256):
            r.wait("sync", "vector", yT_done)
            r.dma("sync", g.dbg4_sT, slotsT[:, :], slot="p4dbg")

        r.wait("tensor", "vector", yT_done)
        for j in range(16):
            for c in range(CT):
                tpb = g.tpb[c % 2]
                tp_gate4[c % 2].gate(r, "tensor")
                tc = r.op("tensor", "transpose", tpb[:, 0:128],
                          yT[:, c * NP + j * 128:c * NP + (j + 1) * 128],
                          g.ident_bf[:, :])
                bg("yT").use("tensor", tc)
                r.wait("vector", "tensor", tc)
                vc = r.op("vector", "tensor_copy",
                          ysb[:, j * 1024 + c * 128:j * 1024 + (c + 1) * 128],
                          tpb[:, 0:128])
                tp_gate4[c % 2].use("vector", vc)
        y_done = r.cnt["vector"]
        r.wait("tensor", "vector", y_done)
        if g.dbg and (g.dbgmask & 32):
            r.wait("sync", "vector", y_done)
            for j in range(16):
                r.dma("sync", g.dbg4_y[j * 128:(j + 1) * 128, :],
                      ysb[:, j * 1024:(j + 1) * 1024], slot="p4dbg")
        r.join()

    with ExitStack() as s4:
        comb = sbuf(s4, nc, "p4_comb", (128, NP), BF16)
        combT = sbuf(s4, nc, "p4_combT", (128, 16 * 512), BF16)
        xold = sbuf(s4, nc, "p4_xold", (128, 8 * 512), F32)
        tsb = sbuf(s4, nc, "p4_t", (128, 8 * 512), F32)
        xn4 = sbuf(s4, nc, "p4_xn4", (128, 8 * 512), F32)
        outst = sbuf(s4, nc, "p4_outst", (128, 1024), F32)

        for m5 in range(8):
            m0 = m5 * 512
            bg("combT").gate(r, "vector")
            for i4 in range(4):
                mt = m5 * 4 + i4
                bg("etile").gate(r, "gpsimd")
                ld = r.dma("gpsimd", etile[:, :], g.Eb[mt * 128:mt * 128 + 128, :],
                           slot="p4et")
                r.waitd("vector", ld)
                bg("comb").gate(r, "vector")
                vc = r.op("vector", "tensor_scalar", comb[:, :], etile[:, :],
                          snr[:, mt:mt + 1], None, op0=ALU.mult)
                bg("etile").use("vector", vc)
                r.wait("tensor", "vector", vc)
                for j in range(16):
                    tpb = g.tpb[j % 2]
                    tp_gate4[j % 2].gate(r, "tensor")
                    tc = r.op("tensor", "transpose", tpb[:, 0:128],
                              comb[:, j * 128:(j + 1) * 128], g.ident_bf[:, :])
                    bg("comb").use("tensor", tc)
                    r.wait("vector", "tensor", tc)
                    cc2 = r.op("vector", "tensor_copy",
                               combT[:, j * 512 + i4 * 128:j * 512 + (i4 + 1) * 128],
                               tpb[:, 0:128])
                    tp_gate4[j % 2].use("vector", cc2)
            cT_done = r.cnt["vector"]
            if g.dbg and m5 == 0 and (g.dbgmask & 128):
                r.wait("sync", "vector", cT_done)
                dtk = r.dma("sync", g.dbg4_cT, combT[:, :], slot="p4dbg")
                bg("combT").use_tok(dtk)
            bg("xold").gate(r, "sync")
            for c in range(CT):
                xold_ready = r.dma("sync", xold[:, c * 512:(c + 1) * 512],
                                   min_[c * 128:(c + 1) * 128,
                                        MOFF + m0:MOFF + m0 + 512], slot="p4xold")
            r.wait("tensor", "vector", cT_done)
            r.waitd("vector", xold_ready)
            for et in range(CT):
                pb = 4 + et % 2
                bank_gate[pb].gate(r, "tensor")
                for j in range(16):
                    mmc = r.mm(P[pb][:, :],
                               ysb[:, j * 1024 + et * 128:j * 1024 + (et + 1) * 128],
                               combT[:, j * 512:(j + 1) * 512],
                               start=(j == 0), stop=(j == 15))
                    bg("ysb").use("tensor", mmc)
                    bg("combT").use("tensor", mmc)
                r.wait("vector", "tensor", mmc)
                bg(("tsb", et)).gate(r, "vector")
                tv = r.op("vector", "tensor_tensor", tsb[:, et * 512:(et + 1) * 512],
                          xold[:, et * 512:(et + 1) * 512], P[pb][:, :], ALU.add)
                bank_gate[pb].use("vector", tv)
                bg("xold").use("vector", tv)
                r.wait("sync", "vector", tv)
                if g.dbg and (g.dbgmask & 64):
                    dtt = r.dma("sync", g.dbg4_t[et * 128:(et + 1) * 128,
                                                 m0:m0 + 512],
                                tsb[:, et * 512:(et + 1) * 512], slot="p4dbg")
                    bg(("tsb", et)).use_tok(dtt)
                ln.stats_et(tsb[:, et * 512:(et + 1) * 512], et, tv)

            xn_done = [0]

            def sink(et, v2, xnew_ap, lnk):
                bg("xn4").gate(r, "vector")
                cpc = r.op("vector", "tensor_copy",
                           xn4[:, et * 512:(et + 1) * 512], xnew_ap)
                bg("xn4").use("vector", cpc)
                xn_done[0] = cpc

            def uses(et, v2):
                bg(("tsb", et)).use("vector", v2)
            ln.sink = sink
            ln.finish(tsb, None, 0, tsb_uses=uses)
            r.wait("tensor", "vector", xn_done[0])
            for i4 in range(4):
                for c in range(CT):
                    pb = 4 + c % 2
                    bank_gate[pb].gate(r, "tensor")
                    tc = r.op("tensor", "transpose", P[pb][:, 0:128],
                              xn4[:, c * 512 + i4 * 128:c * 512 + (i4 + 1) * 128],
                              g.ident_f32[:, :])
                    bg("xn4").use("tensor", tc)
                    r.wait("vector", "tensor", tc)
                    bg("outst").gate(r, "vector")
                    vc = r.op("vector", "tensor_copy",
                              outst[:, c * 128:(c + 1) * 128], P[pb][:, 0:128])
                    bank_gate[pb].use("vector", vc)
                r.wait("sync", "vector", vc)
                wr = r.dma("sync", out[m0 + i4 * 128:m0 + (i4 + 1) * 128, :],
                           outst[:, :], slot="p4out")
                bg("outst").use_tok(wr)

    r.flush_dmas()
    r.join()
    for q in range(8):
        fo = r.dma("sync", g.out_ext[q * 512:(q + 1) * 512, :],
                   g.out[q * 512:(q + 1) * 512, :], slot="p4fout")
    r.waitd("sync", fo)
    r.flush_dmas()
    fin = {e_: r.cnt[e_] for e_ in ENGINES}
    for e_ in ENGINES:
        for o in ENGINES:
            if o != e_:
                r.wait(e_, o, fin[o])



def build_program(phases=("p1",), unfused=True, dbg=False, dbgmask=255):
    nc = bass.Bass("TRN2", target_bir_lowering=False,
                   dynamic_dma_scratch_size=16384)
    g = H()
    g.dbg = dbg
    g.dbgmask = dbgmask
    dk = "ExternalOutput" if dbg else "Internal"
    single = unfused and len(phases) == 1

    def master(name, kind):
        return nc.dram_tensor(name, [C, MCOLS], F32, kind=kind).ap()

    if "p1" in phases:
        g.x_in = nc.dram_tensor("x_in", [T, C], F32, kind="ExternalInput").ap()
        g.master_a = master("master_a", dk)
        g.master_b = master("master_b",
                            "ExternalOutput" if single else "Internal")
        g.kT_hbm = nc.dram_tensor("kT_scr", [32, 64, NH * 128], BF16, kind=dk).ap()
        g.v_hbm = nc.dram_tensor("v_scr", [T, C], BF16, kind=dk).ap()
        g.k_hbm = nc.dram_tensor("k_scr", [T, C], BF16, kind=dk).ap()
        g.v2_hbm = nc.dram_tensor("v2_scr", [T, C], BF16).ap()
        if dbg:
            g.dbg_t = nc.dram_tensor("dbg_t", [C, T], F32, kind="ExternalOutput").ap()
            g.dbg_o = nc.dram_tensor("dbg_o", [C, T], BF16, kind="ExternalOutput").ap()
        g.W_attn = nc.dram_tensor("W_attn", [C, 3 * C], F32,
                                  kind="ExternalInput").ap()
        g.W_proj = nc.dram_tensor("W_proj", [C, C], F32, kind="ExternalInput").ap()
    if "p2" in phases:
        if single:
            g.p2_min = master("m_in", "ExternalInput")
            g.p2_mout = master("m_out", "ExternalOutput")
        else:
            g.p2_min, g.p2_mout = g.master_b, g.master_a
        g.gate_wT = nc.dram_tensor("gate_wT", [C, 8], F32, kind="ExternalInput").ap()
        g.W1T = nc.dram_tensor("W1T", [8, C, 1536], F32, kind="ExternalInput").ap()
        g.W2T = nc.dram_tensor("W2T", [8, 768, C], F32, kind="ExternalInput").ap()
        g.Ws1T = nc.dram_tensor("Ws1T", [C, 3072], F32, kind="ExternalInput").ap()
        g.Ws2T = nc.dram_tensor("Ws2T", [1536, C], F32, kind="ExternalInput").ap()
        if dbg:
            g.dbg_wd = nc.dram_tensor("dbg_wd", [T, 8], F32,
                                      kind="ExternalOutput").ap()
            g.dbg_z = nc.dram_tensor("dbg_z", [T, C], BF16,
                                     kind="ExternalOutput").ap()
    if "p3" in phases:
        if single:
            g.p3_min = master("m_in", "ExternalInput")
            g.p3_mout = master("m_out", "ExternalOutput")
        else:
            g.p3_min, g.p3_mout = g.master_a, g.master_b
        g.w_qkv = nc.dram_tensor("w_qkv", [C, 3 * C], F32, kind="ExternalInput").ap()
        g.w_out = nc.dram_tensor("w_out", [C, C], F32, kind="ExternalInput").ap()
        g.wth_kron = nc.dram_tensor("wth_kron", [128, NH * 4 * 512], F32,
                                    kind="ExternalInput").ap()
        g.b_outT = nc.dram_tensor("b_outT", [128, 8], F32, kind="ExternalInput").ap()
        if dbg:
            g.dbg3_o = nc.dram_tensor("dbg3_o", [C, T], BF16,
                                      kind="ExternalOutput").ap()
            g.dbg3_mx = nc.dram_tensor("dbg3_mx", [C, T], BF16,
                                       kind="ExternalOutput").ap()
    if "p4" in phases:
        if single:
            g.p4_min = master("m_in", "ExternalInput")
        else:
            g.p4_min = g.master_b
        g.out_ext = nc.dram_tensor("out", [T, C], F32, kind="ExternalOutput").ap()
        g.out = nc.dram_tensor("out_i", [T, C], F32).ap()
        g.phi_d = nc.dram_tensor("phi", [C, 2048], F32, kind="ExternalInput").ap()
        g.Wl1_d = nc.dram_tensor("Wl1", [64, C, C], F32, kind="ExternalInput").ap()
        g.bT_d = nc.dram_tensor("bT", [128, 8 * 64], F32, kind="ExternalInput").ap()
        g.Eb_t = nc.dram_tensor("Eb", [T, 2048], F32)
        g.Db_t = nc.dram_tensor("Db", [T, 2048], F32)
        if dbg:
            g.dbg4_E = nc.dram_tensor("dbg4_E", [T, 2048], F32,
                                      kind="ExternalOutput").ap()
            g.dbg4_D = nc.dram_tensor("dbg4_D", [T, 2048], F32,
                                      kind="ExternalOutput").ap()
        g.Eb = g.Eb_t.ap()
        g.Db = g.Db_t.ap()
        if dbg:
            g.dbg4_dp = nc.dram_tensor("dbg4_dp", [512, 2048], BF16,
                                       kind="ExternalOutput").ap()
            g.dbg4_xt = nc.dram_tensor("dbg4_xt", [T, 1024], BF16,
                                       kind="ExternalOutput").ap()
            g.dbg4_sl = nc.dram_tensor("dbg4_sl", [2048, C], F32,
                                       kind="ExternalOutput").ap()
            g.dbg4_id = nc.dram_tensor("dbg4_id", [128, 128], BF16,
                                       kind="ExternalOutput").ap()
            g.dbg4_y = nc.dram_tensor("dbg4_y", [2048, C], BF16,
                                      kind="ExternalOutput").ap()
            g.dbg4_wn = nc.dram_tensor("dbg4_wn", [128, 8192], BF16,
                                       kind="ExternalOutput").ap()
            g.dbg4_sT = nc.dram_tensor("dbg4_sT", [128, 8 * 2048], BF16,
                                       kind="ExternalOutput").ap()
            g.dbg4_t = nc.dram_tensor("dbg4_t", [C, T], F32,
                                      kind="ExternalOutput").ap()
            g.dbg4_cT = nc.dram_tensor("dbg4_cT", [128, 8192], BF16,
                                       kind="ExternalOutput").ap()

    with ExitStack() as ctx:
        const_handles = {}
        for name, shape, dtype in CONST_SPECS:
            hnd = ctx.enter_context(
                nc.sbuf_tensor(f"c_{name}", list(shape), dtype))
            const_handles[name] = hnd
            setattr(g, name, hnd)
            setattr(g, name + "_dram",
                    nc.dram_tensor(f"cd_{name}", list(shape),
                                   dtype, kind="ExternalInput").ap())

        g.psum = [ctx.enter_context(nc.psum_tensor(f"pb{i}", [128, 512], F32))
                  for i in range(6)]
        g.tpb = [ctx.enter_context(nc.psum_tensor(f"tpb{i}", [128, 1024], BF16))
                 for i in range(2)]

        sems = {}
        for e in (*ENGINES, "cc"):
            sems[e] = ctx.enter_context(nc.semaphore(f"sem_{e}"))

        def make_sem(name):
            return ctx.enter_context(nc.semaphore(name))

        r = Rec(nc, sems, make_sem)
        for name, shape, dtype in CONST_SPECS:
            ctok = r.dma("sync", const_handles[name][:, :],
                         getattr(g, name + "_dram"), slot="consts")
        for e in ("tensor", "vector", "scalar", "gpsimd"):
            r.waitd(e, ctok)

        for ph in phases:
            with ExitStack() as pctx:
                if ph == "p1":
                    p1_l0_attn(nc, r, pctx, g)
                elif ph == "p2":
                    p2_l0_ffn(nc, r, pctx, g)
                elif ph == "p3":
                    p3_l1_attn(nc, r, pctx, g)
                elif ph == "p4":
                    p4_l1_ffn(nc, r, pctx, g)
                with nc.Block() as block:
                    r.replay(block)

    return nc


def const_inputs():
    hc = host_constants()
    return {f"cd_{name}": np.ascontiguousarray(hc[name])
            for (name, _, _) in CONST_SPECS}


_prog_cache = {}


def _get_prog(phases):
    key = tuple(phases)
    if key not in _prog_cache:
        _prog_cache[key] = build_program(key, unfused=True)
    return _prog_cache[key]


def kernel(**inputs):
    ws = host_prep(inputs)
    consts = const_inputs()
    x = inputs["L0_attn_x"]
    cores = list(range(B))

    nc1 = _get_prog(("p1",))
    in_maps = []
    for b in range(B):
        m = {"x_in": np.ascontiguousarray(x[b]),
             "W_attn": ws["W_attn"], "W_proj": ws["W_proj"]}
        m.update(consts)
        in_maps.append(m)
    res = run_bass_kernel_spmd(nc1, in_maps, core_ids=cores)
    masters = [np.asarray(res.results[b]["master_b"]) for b in range(B)]

    nc2 = _get_prog(("p2",))
    in_maps = []
    for b in range(B):
        m = {"m_in": masters[b], "gate_wT": ws["gate_wT"], "W1T": ws["W1T"],
             "W2T": ws["W2T"], "Ws1T": ws["Ws1T"], "Ws2T": ws["Ws2T"]}
        m.update(consts)
        in_maps.append(m)
    res = run_bass_kernel_spmd(nc2, in_maps, core_ids=cores)
    masters = [np.asarray(res.results[b]["m_out"]) for b in range(B)]

    nc3 = _get_prog(("p3",))
    in_maps = []
    for b in range(B):
        m = {"m_in": masters[b], "w_qkv": ws["w_qkv"], "w_out": ws["w_out"],
             "wth_kron": ws["wth_kron"], "b_outT": ws["b_outT"]}
        m.update(consts)
        in_maps.append(m)
    res = run_bass_kernel_spmd(nc3, in_maps, core_ids=cores)
    masters = [np.asarray(res.results[b]["m_out"]) for b in range(B)]

    nc4 = _get_prog(("p4",))
    in_maps = []
    for b in range(B):
        m = {"m_in": masters[b], "phi": ws["phi"], "Wl1": ws["Wl1"],
             "bT": ws["bT"]}
        m.update(consts)
        in_maps.append(m)
    res = run_bass_kernel_spmd(nc4, in_maps, core_ids=cores)
    return np.stack([np.asarray(res.results[b]["out"]) for b in range(B)])
